# version 22
# baseline (speedup 1.0000x reference)
import contextlib
import numpy as np
import concourse.bass as bass
import concourse.mybir as mybir
from concourse.bass_utils import run_bass_kernel_spmd
from concourse.aot_env import allow_ici_all_to_all

F32 = mybir.dt.float32
BF16 = mybir.dt.bfloat16
AF = mybir.ActivationFunctionType
OP = mybir.AluOpType
AX = mybir.AxisListType.X

NCORE = 8
B = 2
T = 2048
D = 1024
NE = 8
NPP = 32
NP = 2048
DCH = 8
MCH = 16
NPT = 4
EPS_LN = 1e-5
EPS_B = 1e-6

ENGS = ("sync", "scalar", "vector", "tensor", "gpsimd")
SEM_NAMES = ["sxa", "sxb", "sxn", "sev", "sp", "sha", "shb", "s5", "s6",
             "spk", "str", "shta", "shtb", "syw", "soa", "sob",
             "s5b", "s5c", "s5d", "sevb", "syx", "syv", "sy2", "sy3", "sy4",
             "gpa", "gpb", "gwc", "gwd", "cc", "te", "ve", "se", "fnc", "wrm"]


def build():
    nc = bass.Bass("TRN2", target_bir_lowering=False, debug=False,
                   detect_race_conditions=False)

    x_d = nc.dram_tensor("x", [B, T, D], F32, kind="ExternalInput").ap()
    wint_d = nc.dram_tensor("wint", [D, D], F32, kind="ExternalInput").ap()
    wgt_d = nc.dram_tensor("wgt", [D, D], F32, kind="ExternalInput").ap()
    phi_d = nc.dram_tensor("phi", [D, NP], F32, kind="ExternalInput").ap()
    wexp_d = nc.dram_tensor("wexp", [NE, D, D], F32, kind="ExternalInput").ap()
    bin_d = nc.dram_tensor("binh", [128, DCH], F32, kind="ExternalInput").ap()
    bgt_d = nc.dram_tensor("bgth", [128, DCH], F32, kind="ExternalInput").ap()
    lam_d = nc.dram_tensor("lamh", [128, DCH], F32, kind="ExternalInput").ap()
    bexp_d = nc.dram_tensor("bexp", [NE, D], F32, kind="ExternalInput").ap()
    out_d = nc.dram_tensor("out", [B, T, D], F32, kind="ExternalOutput").ap()

    xn_d = nc.dram_tensor("xn_s", [B, T, D], BF16).ap()
    h_d = nc.dram_tensor("h_s", [B, D, T], BF16).ap()
    e_d = nc.dram_tensor("e_s", [B, T, NP], BF16).ap()
    d1_d = nc.dram_tensor("d1_s", [T, NP], BF16).ap()
    p_d = nc.dram_tensor("p_s", [T, NP], F32).ap()
    dn_d = nc.dram_tensor("dn_s", [T, NP], F32).ap()
    sli_b = [nc.dram_tensor(f"sli{b}_s", [NCORE, DCH, 128, NE, NPP], BF16).ap()
             for b in range(B)]
    slo_b = [nc.dram_tensor(f"slo{b}_s", [NCORE, DCH, 128, NE, NPP], BF16).ap()
             for b in range(B)]
    yi_h = [nc.dram_tensor(f"yi{h}_s", [NCORE, 4, B, NPP, D], BF16).ap()
            for h in range(2)]
    yo_h = [nc.dram_tensor(f"yo{h}_s", [NCORE, 4, B, NPP, D], BF16).ap()
            for h in range(2)]
    yi_q = [nc.dram_tensor(f"yiq{h}_s", [NCORE, 2, B, NPP, D], BF16).ap()
            for h in range(2)]
    yo_q = [nc.dram_tensor(f"yoq{h}_s", [NCORE, 2, B, NPP, D], BF16).ap()
            for h in range(2)]

    A = nc.alloc_sbuf_tensor("arena_a", [128, 32768], BF16)
    xnfm = A.ap().rearrange("q (b c t) -> q b c t", b=B, c=DCH)
    dsb = A.ap().rearrange("q (m n) -> q m n", m=MCH)
    packed = A.ap().rearrange("q (c w) -> q c w", c=DCH)
    packed6 = A.ap().rearrange("q (c n s b p) -> q c n s b p", c=DCH, n=NE, s=NCORE, b=B)
    etsb = A.ap().rearrange("q (k t) -> q k t", k=MCH)
    Br = nc.alloc_sbuf_tensor("arena_b", [128, 16384], BF16)
    phisb = Br.ap().rearrange("q (c n) -> q c n", c=DCH)
    xntok = Br.ap().rearrange("q (m d) -> q m d", m=MCH)
    wbuf = Br.ap().rearrange("q (w c e) -> q w c e", w=2, c=DCH)
    ysb = Br.ap().rearrange("q (k e) -> q k e", k=MCH)
    C = nc.alloc_sbuf_tensor("arena_c", [128, 16384], BF16)
    cc2 = C.ap().rearrange("q (h c n) -> q h c n", h=2, c=DCH)
    wisb = cc2[:, 0]
    wgsb = cc2[:, 1]

    est = nc.alloc_sbuf_tensor("est", [128, B, 2, 512], BF16).ap()
    hstage = nc.alloc_sbuf_tensor("hstage", [128, 2, T], BF16).ap()
    htt = nc.alloc_sbuf_tensor("htt", [128, 2, D], BF16).ap()
    slstage = nc.alloc_sbuf_tensor("slstage", [128, DCH, 512], BF16).ap()
    ystage = slstage.rearrange("q c n -> q (c n)")[:, 0:D]
    d1st = hstage
    onesb = nc.alloc_sbuf_tensor("onesb", [1, 128], BF16).ap()
    bexpr = nc.alloc_sbuf_tensor("bexpr", [1, 2 * D], BF16).ap()

    R = nc.alloc_sbuf_tensor("arena_r", [128, 5 * T], F32)
    rows = [R.ap()[:, i * T:(i + 1) * T] for i in range(5)]
    dnrow = R.ap()[:, 0:NP]
    rrow = R.ap()[:, NP:2 * NP]
    obuf = R.ap().rearrange("q (k t) -> q k t", k=5)[:, 0:4, 0:D]

    xt = R.ap().rearrange("q (k t) -> q k t", k=5)[:, 0:4, D:2 * D]
    pst = nc.alloc_sbuf_tensor("pst", [128, 512], F32).ap()
    accs = nc.alloc_sbuf_tensor("accs", [128, 128], F32).ap()
    dcs = nc.alloc_sbuf_tensor("dcs", [128, 32], F32).ap()
    dcr = nc.alloc_sbuf_tensor("dcr", [128, 32], F32).ap()
    stat = nc.alloc_sbuf_tensor("stat", [128, 64], F32).ap()
    smls = nc.alloc_sbuf_tensor("smls", [128, 4 * DCH], F32).ap()
    s_half = smls[:, 0:DCH]
    bih = smls[:, DCH:2 * DCH]
    bgh = smls[:, 2 * DCH:3 * DCH]
    stmp = smls[:, 3 * DCH:4 * DCH]
    binsb = nc.alloc_sbuf_tensor("binsb", [128, DCH], F32).ap()
    bgtsb = nc.alloc_sbuf_tensor("bgtsb", [128, DCH], F32).ap()
    lamsb = nc.alloc_sbuf_tensor("lamsb", [128, DCH], F32).ap()

    psum = [nc.alloc_psum_tensor(f"ps{i}", [128, 512], F32) for i in range(8)]
    psa = [p.ap() for p in psum]

    plan = {e: [] for e in ENGS}
    sems = {}
    cnt = {s: 0 for s in SEM_NAMES}

    def emit(eng, fn):
        plan[eng].append(fn)

    def inc(s, n):
        cnt[s] += n
        return cnt[s]

    def snap(s):
        return cnt[s]

    def barrier(eng, waits):
        waits = tuple(w for w in waits if w is not None)

        def f(e, waits=waits):
            for (s, v) in waits:
                e.wait_ge(sems[s], v)
        emit(eng, f)

    bank_prod = [None] * 8
    bank_cons = [None] * 8

    def mm_group(bank, mms):
        wait = bank_cons[bank]

        def f(t, wait=wait, mms=mms, bank=bank):
            if wait is not None:
                t.wait_ge(sems[wait[0]], wait[1])
            n = len(mms)
            for i, (lf, rf) in enumerate(mms):
                ins = t.matmul(psa[bank][:, :], lf(), rf(),
                               start=(i == 0), stop=(i == n - 1))
            ins.then_inc(sems["te"], 1)
        emit("tensor", f)
        bank_prod[bank] = ("te", inc("te", 1))

    def mm_group_multi(banks_mms):
        waits = [bank_cons[b] for b, _ in banks_mms]

        def f(t, waits=tuple(waits), banks_mms=banks_mms):
            for w in waits:
                if w is not None:
                    t.wait_ge(sems[w[0]], w[1])
            nk = len(banks_mms[0][1])
            last = None
            for i in range(nk):
                for b, mms in banks_mms:
                    lf, rf = mms[i]
                    last = t.matmul(psa[b][:, :], lf(), rf(),
                                    start=(i == 0), stop=(i == nk - 1))
            last.then_inc(sems["te"], 1)
        emit("tensor", f)
        v = inc("te", 1)
        for b, _ in banks_mms:
            bank_prod[b] = ("te", v)

    def consume(bank, eng, fn, sem, waits=()):
        prod = bank_prod[bank]

        def f(e, prod=prod, fn=fn, waits=tuple(w for w in waits if w is not None)):
            for (s, v) in waits:
                e.wait_ge(sems[s], v)
            if prod is not None:
                e.wait_ge(sems[prod[0]], prod[1])
            fn(e).then_inc(sems[sem], 1)
        emit(eng, f)
        bank_cons[bank] = (sem, inc(sem, 1))

    def dma(eng, sem, out_fn, in_fn, waits=(), transpose=False):
        def f(e, waits=tuple(w for w in waits if w is not None),
              transpose=transpose):
            for (s, v) in waits:
                e.wait_ge(sems[s], v)
            if transpose:
                e.dma_start_transpose(out_fn(), in_fn()).then_inc(sems[sem], 16)
            else:
                e.dma_start(out=out_fn(), in_=in_fn()).then_inc(sems[sem], 16)
        emit(eng, f)
        return inc(sem, 16)

    def spacer(eng):
        emit(eng, lambda e: e.drain())

    def ew(eng, sem, fn, waits=()):
        def f(e, waits=tuple(w for w in waits if w is not None), fn=fn):
            for (s, v) in waits:
                e.wait_ge(sems[s], v)
            fn(e).then_inc(sems[sem], 1)
        emit(eng, f)
        return inc(sem, 1)

    with contextlib.ExitStack() as stack:
        for s in SEM_NAMES:
            sems[s] = stack.enter_context(nc.semaphore(s))
        ids = [sems[s].num for s in SEM_NAMES]
        srange = range(min(ids), max(ids) + 1)
        nc.gpsimd.dma_reset(srange)
        nc.gpsimd.sem_clear(srange)
        nc.all_engine_barrier()
        block = stack.enter_context(nc.Block())

        def fwarm_gp(e):
            for k in range(4):
                e.dma_start(out=lamsb[:, :], in_=lam_d[:, :]).then_inc(
                    sems["fnc"], 16)
                e.wait_ge(sems["fnc"], 16 * (k + 1))
            e.sem_inc(sems["wrm"], 1)
        emit("gpsimd", fwarm_gp)
        cnt["fnc"] += 64
        cnt["wrm"] += 1
        for eng in ("sync", "scalar", "vector", "tensor"):
            def fwarm(e):
                e.wait_ge(sems["wrm"], 1)
            emit(eng, fwarm)

        dma("sync", "s5", lambda: binsb[:, :], lambda: bin_d[:, :])
        dma("sync", "s5", lambda: bgtsb[:, :], lambda: bgt_d[:, :])
        v_small = dma("sync", "s5", lambda: lamsb[:, :], lambda: lam_d[:, :])
        v_phi = dma("gpsimd", "gpa", lambda: phisb[:, :, :],
                    lambda: phi_d.rearrange("(c p) n -> p c n", p=128))
        v_wi = dma("gpsimd", "gpa", lambda: wisb[:, :, :],
                   lambda: wint_d.rearrange("(c p) n -> p c n", p=128))
        v_wg = dma("gpsimd", "gpa", lambda: wgsb[:, :, :],
                   lambda: wgt_d.rearrange("(c p) n -> p c n", p=128))
        ew("scalar", "se", lambda e: e.activation(stmp[:, :], lamsb[:, :], AF.Exp),
           waits=[("s5", v_small)])
        spacer("scalar")
        ew("scalar", "se", lambda e: e.add(stmp[:, :], stmp[:, :], 1.0))
        spacer("scalar")
        ew("scalar", "se", lambda e: e.activation(stmp[:, :], stmp[:, :], AF.Ln))
        spacer("scalar")
        p0_se = ew("scalar", "se", lambda e: e.mul(s_half[:, :], stmp[:, :], -4.0))
        ew("vector", "ve", lambda e: e.tensor_scalar_mul(bih[:, :], binsb[:, :], 0.5),
           waits=[("s5", v_small)])
        ew("vector", "ve", lambda e: e.tensor_scalar_mul(bgh[:, :], bgtsb[:, :], 0.5))
        ew("vector", "ve", lambda e: e.memset(onesb[:, :], 1.0))

        xloadp, xapply, sq_of, sd_of, xn_wrp = {}, {}, {}, {}, {}
        sxp = ["sxa", "sxb"]
        p1_b_done = {}

        def ln_chunk(i, slot, b, mc):
            ss = stat[:, 8 * (i % 4):8 * (i % 4) + 8]
            pr = i // 2
            wr = [("ve", xapply[i - 4]), ("se", sd_of[i - 4])] if i >= 4 else []
            ew("vector", "ve",
               lambda e, slot=slot, ss=ss: e.reduce_sum(ss[:, 0:1], xt[:, slot, :],
                                                        axis=AX),
               waits=wr + [(sxp[pr % 2], xloadp[pr])])
            sq_of[i] = ew("scalar", "se",
                          lambda e, slot=slot, ss=ss: e.activation(
                              hstage[:, 0, 0:D], xt[:, slot, :], AF.Square,
                              accum_out=ss[:, 1:2]),
                          waits=([("se", sd_of[i - 4])] if i >= 4 else []) +
                          [(sxp[pr % 2], xloadp[pr])])
            spacer("vector")
            ew("vector", "ve",
               lambda e, ss=ss: e.tensor_scalar_mul(ss[:, 2:3], ss[:, 0:1], 1.0 / D),
               waits=[("se", sq_of[i])])
            spacer("vector")
            ew("vector", "ve",
               lambda e, ss=ss: e.tensor_mul(ss[:, 3:4], ss[:, 2:3], ss[:, 2:3]))
            ew("vector", "ve",
               lambda e, ss=ss: e.tensor_scalar(ss[:, 4:5], ss[:, 1:2], 1.0 / D,
                                                float(EPS_LN), OP.mult, OP.add))
            spacer("vector")
            ew("vector", "ve",
               lambda e, ss=ss: e.tensor_sub(ss[:, 5:6], ss[:, 4:5], ss[:, 3:4]))
            sd_of[i] = ew("scalar", "se",
                          lambda e, ss=ss: e.activation(ss[:, 6:7], ss[:, 5:6],
                                                        AF.Sqrt),
                          waits=[("ve", snap("ve"))])
            ew("vector", "ve",
               lambda e, ss=ss: e.reciprocal(ss[:, 7:8], ss[:, 6:7]),
               waits=[("se", sd_of[i])])
            spacer("vector")
            k = i % 2
            wa = [("sxn", xn_wrp[pr - 1])] if pr >= 1 else []
            xapply[i] = ew("vector", "ve",
                           lambda e, slot=slot, ss=ss, k=k: e.tensor_scalar(
                               hstage[:, 1, k * D:(k + 1) * D], xt[:, slot, :],
                               ss[:, 2:3], ss[:, 7:8],
                               OP.subtract, OP.mult),
                           waits=wa)

        def ln_pair(pr, b, pc):
            w = []
            if pr >= 2:
                w = [("ve", xapply[2 * (pr - 2) + 1]), ("se", sq_of[2 * (pr - 2) + 1])]
            xloadp[pr] = dma(
                "sync", sxp[pr % 2],
                lambda pr=pr: xt[:, 2 * (pr % 2):2 * (pr % 2) + 2, :],
                lambda b=b, pc=pc: x_d[b, pc * 256:(pc + 1) * 256, :].rearrange(
                    "(a q) d -> q a d", q=128),
                waits=w)
            ln_chunk(2 * pr, 2 * (pr % 2), b, 2 * pc)
            ln_chunk(2 * pr + 1, 2 * (pr % 2) + 1, b, 2 * pc + 1)
            xn_wrp[pr] = dma(
                "sync", "sxn",
                lambda b=b, pc=pc: xn_d[b, pc * 256:(pc + 1) * 256, :].rearrange(
                    "(a q) d -> q a d", q=128),
                lambda: hstage[:, 1, 0:2 * D].rearrange("q (a d) -> q a d", a=2),
                waits=[("ve", xapply[2 * pr + 1])])

        pr = 0
        for b in range(B):
            for pc in range(8):
                ln_pair(pr, b, pc)
                pr += 1
            if b == 0:
                ln_pair(pr, 0, 0)
                pr += 1
                p1_b_done[0] = snap("sxn")
                for c in range(DCH):
                    dma("scalar", "str",
                        lambda c=c: xnfm[:, 0, c, :],
                        lambda c=c: xn_d[0, :, c * 128:(c + 1) * 128],
                        waits=([("sxn", p1_b_done[0])] if c == 0 else []),
                        transpose=True)
        p1_sxn = snap("sxn")

        for c in range(DCH):
            dma("scalar", "str",
                lambda c=c: xnfm[:, 1, c, :],
                lambda c=c: xn_d[1, :, c * 128:(c + 1) * 128],
                waits=([("sxn", p1_sxn)] if c == 0 else []),
                transpose=True)
        p2_done = snap("str")

        barrier("tensor", [("gpa", v_wg), ("str", p2_done)])
        gi = 0
        prev_p_wr = {}
        est_free = {}
        for mc in range(MCH):
            for npt in range(NPT):
                par = npt % 2
                for b in range(B):
                    bank = gi % 4
                    gi += 1
                    mms = []
                    for kc in range(DCH):
                        mms.append((
                            lambda b=b, kc=kc, mc=mc:
                            xnfm[:, b, kc, mc * 128:(mc + 1) * 128],
                            lambda kc=kc, npt=npt:
                            phisb[:, kc, npt * 512:(npt + 1) * 512],
                        ))
                    mm_group(bank, mms)
                    col = b * 64 + mc * 4 + npt
                    if b == 0:
                        eng_e, sevp = "sync", ("sev" if par == 0 else "sevb")
                    else:
                        eng_e, sevp = "gpsimd", ("gpa" if par == 0 else "gpb")
                    consume(bank, "scalar",
                            lambda e, b=b, col=col, bank=bank, par=par:
                            e.activation(est[:, b, par, :], psa[bank][:, :], AF.Exp,
                                         accum_out=accs[:, col:col + 1]),
                            "se", waits=list(est_free.get((b, par), ())))
                    dma(eng_e, sevp,
                        lambda b=b, mc=mc, npt=npt:
                        e_d[b, mc * 128:(mc + 1) * 128,
                            npt * 512:(npt + 1) * 512],
                        lambda b=b, par=par: est[:, b, par, :],
                        waits=[("se", snap("se"))])
                pv = ew("vector", "ve",
                        lambda e, par=par: e.tensor_add(pst[:, :], est[:, 0, par, :],
                                                        est[:, 1, par, :]),
                        waits=[("se", snap("se")),
                               ("sp", prev_p_wr.get(0)) if prev_p_wr.get(0) else None])
                prev_p_wr[0] = dma("scalar", "sp",
                                   lambda mc=mc, npt=npt:
                                   p_d[mc * 128:(mc + 1) * 128,
                                       npt * 512:(npt + 1) * 512],
                                   lambda: pst[:, :], waits=[("ve", pv)])
                for b in range(B):
                    sv = ("sev" if par == 0 else "sevb") if b == 0 else \
                         ("gpa" if par == 0 else "gpb")
                    est_free[(b, par)] = [(sv, snap(sv)), ("ve", pv)]
        p4_sev = snap("sev")
        p4_sevb = snap("sevb")
        p4_gpa = snap("gpa")
        p4_gpb = snap("gpb")
        for b in range(B):
            for mc in range(MCH):
                col = b * 64 + mc * 4
                ew("vector", "ve",
                   lambda e, col=col, b=b, mc=mc: e.reduce_sum(
                       dcs[:, b * 16 + mc:b * 16 + mc + 1], accs[:, col:col + 4],
                       axis=AX),
                   waits=([("se", snap("se"))] if (b == 0 and mc == 0) else []))
        spacer("vector")
        ew("vector", "ve", lambda e: e.reciprocal(dcr[:, :], dcs[:, :]))

        p4_sp = snap("sp")

        def f_ar(g, p4_sp=p4_sp):
            g.wait_ge(sems["sp"], p4_sp)
            g.collective_compute(
                "AllReduce", OP.add, replica_groups=[list(range(NCORE))],
                ins=[p_d.opt()], outs=[dn_d.opt()],
            ).then_inc(sems["cc"], 1)
        emit("gpsimd", f_ar)
        ar_done = inc("cc", 1)

        xn0_last = None
        for mc in range(MCH):
            xn0_last = dma("sync", "s6", lambda mc=mc: xntok[:, mc, :],
                           lambda mc=mc: xn_d[0, mc * 128:(mc + 1) * 128, :],
                           waits=([("te", snap("te"))] if mc == 0 else []))

        barrier("tensor", [("gpa", v_wg)])
        barrier("scalar", [("ve", 3), ("se", p0_se)])
        pair = 0
        h_wr = {}
        shp = ["sha", "shb"]
        prev_t2 = None
        scan_of = {}
        for b in range(B):
            for dc in range(DCH):
                ci = b * DCH + dc
                if prev_t2 is not None:
                    barrier("scalar", [("ve", prev_t2)])
                tanh_done = []
                for mt in range(4):
                    bg = 2 + (pair % 2) * 2
                    bi_ = bg + 1
                    pair += 1
                    mmg = [(lambda kc=kc, dc=dc: wgsb[:, kc, dc * 128:(dc + 1) * 128],
                            lambda b=b, kc=kc, mt=mt:
                            xnfm[:, b, kc, mt * 512:(mt + 1) * 512])
                           for kc in range(DCH)]
                    mmi = [(lambda kc=kc, dc=dc: wisb[:, kc, dc * 128:(dc + 1) * 128],
                            lambda b=b, kc=kc, mt=mt:
                            xnfm[:, b, kc, mt * 512:(mt + 1) * 512])
                           for kc in range(DCH)]
                    mm_group(bg, mmg)
                    mm_group(bi_, mmi)
                    consume(bg, "scalar",
                            lambda e, bg=bg, mt=mt, dc=dc: e.activation(
                                rows[0][:, mt * 512:(mt + 1) * 512], psa[bg][:, :],
                                AF.Tanh, scale=0.5, bias=bgh[:, dc:dc + 1]),
                            "se")
                    consume(bi_, "scalar",
                            lambda e, bi_=bi_, mt=mt, dc=dc: e.activation(
                                rows[1][:, mt * 512:(mt + 1) * 512], psa[bi_][:, :],
                                AF.Tanh, scale=0.5, bias=bih[:, dc:dc + 1]),
                            "se")
                    tanh_done.append(snap("se"))
                for hf in range(2):
                    sl = slice(hf * 1024, (hf + 1) * 1024)
                    av = ew("scalar", "se",
                            lambda e, sl=sl, dc=dc: e.activation(
                                rows[2][:, sl], rows[0][:, sl], AF.Exp,
                                scale=s_half[:, dc:dc + 1], bias=s_half[:, dc:dc + 1]),
                            waits=[("se", tanh_done[2 * hf + 1])] +
                            ([("ve", scan_of[(ci - 1, hf)])] if ci >= 1 else []))
                    ew("vector", "ve",
                       lambda e, sl=sl: e.tensor_mul(rows[3][:, sl], rows[2][:, sl],
                                                     rows[2][:, sl]),
                       waits=[("se", av)])
                    ew("vector", "ve",
                       lambda e, sl=sl: e.tensor_scalar(rows[4][:, sl], rows[3][:, sl],
                                                        -1.0, 1.0 + EPS_B,
                                                        OP.mult, OP.add))
                bsq = []
                for hf in range(2):
                    sl = slice(hf * 1024, (hf + 1) * 1024)
                    bsq.append(ew("scalar", "se",
                                  lambda e, sl=sl: e.activation(
                                      rows[3][:, sl], rows[4][:, sl], AF.Sqrt),
                                  waits=[("ve", snap("ve"))]))
                for hf in range(2):
                    sl = slice(hf * 1024, (hf + 1) * 1024)
                    ew("vector", "ve",
                       lambda e, sl=sl: e.tensor_scalar(rows[4][:, sl], rows[1][:, sl],
                                                        0.5, 0.5, OP.mult, OP.add),
                       waits=[("se", bsq[hf])])
                    tv = ew("vector", "ve",
                            lambda e, sl=sl: e.tensor_mul(rows[1][:, sl],
                                                          rows[4][:, sl],
                                                          rows[3][:, sl]))
                    if hf == 1:
                        prev_t2 = tv
                for hf in range(2):
                    sl = slice(hf * 1024, (hf + 1) * 1024)
                    ew("vector", "ve",
                       lambda e, sl=sl, b=b, dc=dc: e.tensor_mul(
                           rows[4][:, sl], rows[1][:, sl], xnfm[:, b, dc, sl]))
                hp = ci % 2
                wv_ = [(shp[hp], h_wr[ci - 2])] if ci >= 2 else []
                scan_of[(ci, 0)] = ew(
                    "vector", "ve",
                    lambda e, hp=hp: e.tensor_tensor_scan(
                        hstage[:, hp, 0:1024], rows[2][:, 0:1024], rows[4][:, 0:1024],
                        0.0, OP.mult, OP.add),
                    waits=wv_)
                spacer("vector")
                scan_of[(ci, 1)] = ew(
                    "vector", "ve",
                    lambda e, hp=hp: e.tensor_tensor_scan(
                        hstage[:, hp, 1024:2048], rows[2][:, 1024:2048],
                        rows[4][:, 1024:2048],
                        hstage[:, hp, 1023:1024], OP.mult, OP.add))
                h_wr[ci] = dma("sync", shp[hp],
                               lambda b=b, dc=dc: h_d[b, dc * 128:(dc + 1) * 128, :],
                               lambda hp=hp: hstage[:, hp, :],
                               waits=[("ve", scan_of[(ci, 1)])])
        p3_ve = snap("ve")
        p3_sha, p3_shb = snap("sha"), snap("shb")

        dn_t = [rows[0], rows[1], rows[2], rows[3]]
        e0_t = [hstage[:, 0, :], hstage[:, 1, :]]
        slflat = slstage.rearrange("q c n -> q (c n)")
        e1_t = [slflat[:, 0:NP], slflat[:, NP:2 * NP]]
        s5p_n = ["s5", "s5b", "s5c", "s5d"]
        eld_n = ["sha", "shb"]
        rv_of, d1v_of, d1wr_of = {}, {}, {}

        def emit_d1wr_odd(md):
            d1wr_of[md] = ("gwc", dma(
                "gpsimd", "gwc",
                lambda md=md: d1_d[md * 128:(md + 1) * 128, :],
                lambda md=md: e0_t[md % 2],
                waits=[("ve", d1v_of[md])]))

        for mc in range(MCH):
            p = mc % 2
            q4 = mc % 4
            if mc < 4:
                wl = [("cc", ar_done), ("ve", p3_ve),
                      ("sha", p3_sha), ("shb", p3_shb),
                      ("sev", p4_sev), ("sevb", p4_sevb)]
            else:
                wl = [("ve", d1v_of[mc - 4])]
            ldn = dma("sync", s5p_n[q4], lambda q4=q4: dn_t[q4][:, :],
                      lambda mc=mc: dn_d[mc * 128:(mc + 1) * 128, :], waits=wl)
            if mc % 2 == 1 and mc >= 3:
                emit_d1wr_odd(mc - 2)
            if mc < 2:
                wg = [("ve", p3_ve), ("sha", p3_sha), ("shb", p3_shb),
                      ("sev", p4_sev), ("sevb", p4_sevb),
                      ("gpa", p4_gpa), ("gpb", p4_gpb)]
            else:
                wg = [d1wr_of[mc - 2]]
            le0 = dma("scalar", eld_n[p], lambda p=p: e0_t[p],
                      lambda mc=mc: e_d[0, mc * 128:(mc + 1) * 128, :], waits=wg)
            le1 = dma("scalar", eld_n[p], lambda p=p: e1_t[p],
                      lambda mc=mc: e_d[1, mc * 128:(mc + 1) * 128, :])
            ew("scalar", "se",
               lambda e, q4=q4: e.activation(dn_t[q4][:, :], dn_t[q4][:, :], AF.Ln),
               waits=[(s5p_n[q4], ldn)])
            rv_of[mc] = ew("scalar", "se",
                           lambda e, q4=q4: e.activation(dn_t[q4][:, :],
                                                         dn_t[q4][:, :],
                                                         AF.Exp, scale=-1.0))
            ew("vector", "ve",
               lambda e, mc=mc, p=p, q4=q4: e.tensor_mul(dsb[:, mc, :], e0_t[p],
                                                         dn_t[q4][:, :]),
               waits=[("se", rv_of[mc]), (eld_n[p], le1)])
            d1v_of[mc] = ew("vector", "ve",
                            lambda e, p=p, q4=q4: e.tensor_mul(e0_t[p], e1_t[p],
                                                               dn_t[q4][:, :]))
            if mc % 2 == 0:
                d1wr_of[mc] = ("gwd", dma(
                    "gpsimd", "gwd",
                    lambda mc=mc: d1_d[mc * 128:(mc + 1) * 128, :],
                    lambda p=p: e0_t[p], waits=[("ve", d1v_of[mc])]))
        emit_d1wr_odd(15)
        p5_s5 = snap("s5")
        p5_ve = snap("ve")

        def emit_slots(batch, sname, first_waits):
            gi2 = 0
            stage_guard = None
            for npt in range(NPT):
                for dc in range(DCH):
                    bank = 6 + (gi2 % 2)
                    gi2 += 1
                    mms = [(lambda m=m, dc=dc: xntok[:, m, dc * 128:(dc + 1) * 128],
                            lambda m=m, npt=npt: dsb[:, m, npt * 512:(npt + 1) * 512])
                           for m in range(MCH)]
                    if npt == 0 and dc == 0:
                        barrier("tensor", first_waits)
                    mm_group(bank, mms)
                    consume(bank, "vector",
                            lambda e, bank=bank, dc=dc:
                            e.tensor_copy(slstage[:, dc, :], psa[bank][:, :]),
                            "ve",
                            waits=([stage_guard] if (stage_guard and dc == 0) else []))
                ev_last = snap("ve")
                last = None
                for j in range(2):
                    dest = 2 * npt + j
                    last = dma(
                        "sync", sname,
                        lambda dest=dest, batch=batch:
                        sli_b[batch][dest].rearrange("c q n p -> q c (n p)"),
                        lambda j=j: slstage[:, :, j * 256:(j + 1) * 256],
                        waits=[("ve", ev_last)])
                stage_guard = (sname, last)

        emit_slots(0, "s6", [("s6", xn0_last), ("ve", p5_ve)])
        s6_b0 = snap("s6")

        def f_a2a_s0(g):
            g.wait_ge(sems["s6"], s6_b0)
            with allow_ici_all_to_all():
                g.collective_compute(
                    "AllToAll", OP.bypass, replica_groups=[list(range(NCORE))],
                    ins=[sli_b[0].opt()], outs=[slo_b[0].opt()],
                ).then_inc(sems["cc"], 1)
        emit("gpsimd", f_a2a_s0)
        a2as0_done = inc("cc", 1)

        ld_w = [("te", snap("te")), ("gwd", snap("gwd")), ("gwc", snap("gwc"))]
        dma("sync", "spk",
            lambda: dsb[:, 0:8, :],
            lambda: d1_d.rearrange("(m q) n -> q m n", q=128)[:, 0:8, :],
            waits=ld_w)
        xn1_ld = dma("sync", "spk",
                     lambda: xntok[:, 0:8, :],
                     lambda: xn_d[1].rearrange("(m q) d -> q m d", q=128)[:, 0:8, :])
        dma("scalar", "shta",
            lambda: dsb[:, 8:16, :],
            lambda: d1_d.rearrange("(m q) n -> q m n", q=128)[:, 8:16, :],
            waits=ld_w)
        xn1_lds = dma("scalar", "shta",
                      lambda: xntok[:, 8:16, :],
                      lambda: xn_d[1].rearrange("(m q) d -> q m d", q=128)[:, 8:16, :])
        emit_slots(1, "spk", [("spk", xn1_ld), ("shta", xn1_lds)])
        spk_b1 = snap("spk")
        te_after_p6 = snap("te")

        def f_a2a_s1(g):
            g.wait_ge(sems["spk"], spk_b1)
            with allow_ici_all_to_all():
                g.collective_compute(
                    "AllToAll", OP.bypass, replica_groups=[list(range(NCORE))],
                    ins=[sli_b[1].opt()], outs=[slo_b[1].opt()],
                ).then_inc(sems["cc"], 1)
        emit("gpsimd", f_a2a_s1)
        a2as_done = inc("cc", 1)

        first_q = {}
        qi = 0
        pk_last = {}
        for bq in range(B):
            for s in range(NCORE):
                for c in range(DCH):
                    eng, sem = (("sync", "spk"), ("scalar", "str"),
                                ("gpsimd", "gwc"))[qi % 3]
                    qi += 1
                    w = []
                    if (eng, bq) not in first_q:
                        w = [("cc", a2as0_done if bq == 0 else a2as_done),
                             ("te", te_after_p6)]
                        first_q[(eng, bq)] = True
                    v = dma(eng, sem,
                            lambda s=s, c=c, bq=bq: packed6[:, c, :, s, bq, :],
                            lambda s=s, c=c, bq=bq: slo_b[bq][s, c, :, :, :],
                            waits=w)
                    pk_last[sem] = v
        pk_sy = pk_last.get("spk", 0)
        pk_tr = pk_last.get("str", 0)
        pk_gw = pk_last.get("gwc", 0)
        gpp = ["gpa", "gpb"]
        wslot = [wisb, wgsb, wbuf[:, 0], wbuf[:, 1]]
        wstage = [rows[k][:, 0:D] for k in range(4)]
        wssem = ["sxn", "shta", "shtb", "sob"]
        wseng = ["sync", "scalar", "sync", "scalar"]
        cast_of = {}
        wld_of = {}
        p5_rows_free = ("ve", snap("ve"))

        def emit_wchunk(n, kc):
            i = n * 8 + kc
            s4 = i % 4
            wld_of[i] = dma(
                wseng[s4], wssem[s4],
                lambda s4=s4: wstage[s4],
                lambda n=n, kc=kc: wexp_d[n, kc * 128:(kc + 1) * 128, :],
                waits=[("se", cast_of[i - 4]) if i >= 4 else p5_rows_free])
            if kc == 0:
                barrier("vector", [("te", ym_te_end.get(n - 4, te_after_p6))])
            cast_of[i] = ew("vector", "se",
                            lambda e, n=n, kc=kc, s4=s4: e.tensor_copy(
                                wslot[n % 4][:, kc, :], wstage[s4]),
                            waits=[(wssem[s4], wld_of[i])])

        ym_te_end = {}
        bv_d = {}
        for n in range(2):
            emit_wchunk(n, 0) if False else None
        for n in range(4):
            for kc in range(DCH):
                emit_wchunk(n, kc)
        for n in range(2):
            bv_d[n] = dma("gpsimd", gpp[n % 2],
                          lambda n=n: bexpr[0:1, (n % 2) * D:(n % 2 + 1) * D],
                          lambda n=n: bexp_d[n:n + 1, :])
        ysb0 = C.ap().rearrange("q (k e) -> q k e", k=MCH)
        ysb1 = ysb
        eth = A.ap().rearrange("q (h k t) -> q h k t", h=2, k=MCH)

        ystg = [ystage, slflat[:, D:2 * D]]
        y_wr = {}
        gi3 = 0
        for n in range(NE):
            barrier("tensor", [("se", cast_of[n * 8 + 7]), (gpp[n % 2], bv_d[n])] +
                    ([("spk", pk_sy), ("str", pk_tr), ("gwc", pk_gw)]
                     if n == 0 else []))
            for ic in range(4):
                gi = n * 4 + ic
                b0 = (gi3 % 2) * 2
                gi3 += 1
                pairs = []
                for j in range(2):
                    mms = [(lambda n=n, ic=ic, kc=kc:
                            packed[:, kc, n * 512 + ic * 128:n * 512 + (ic + 1) * 128],
                            lambda n=n, kc=kc, j=j:
                            wslot[n % 4][:, kc, j * 512:(j + 1) * 512])
                           for kc in range(DCH)]
                    mms.append((
                        lambda: onesb[0:1, :],
                        lambda n=n, j=j: bexpr[0:1, (n % 2) * D + j * 512:
                                               (n % 2) * D + (j + 1) * 512],
                    ))
                    pairs.append((b0 + j, mms))
                mm_group_multi(pairs)
                prev_wr = y_wr.get(gi - 2)
                for j in range(2):
                    consume(b0 + j, "vector",
                            lambda e, bk=b0 + j, j=j, buf=gi % 2:
                            e.tensor_copy(ystg[buf][:, j * 512:(j + 1) * 512],
                                          psa[bk][:, :]),
                            "ve",
                            waits=([prev_wr] if (j == 0 and prev_wr) else []))
                ev_val = snap("ve")
                if gi % 2 == 0:
                    yeng = "scalar"
                    ysem = "syv" if n < 4 else ("sy2" if n < 6 else "sy3")
                else:
                    yeng = "sync"
                    ysem = "syw" if n < 4 else ("syx" if n < 6 else "sy4")
                for dd in range(2):
                    dest = 2 * ic + dd
                    v = dma(
                        yeng, ysem,
                        lambda dest=dest, n=n:
                        (yi_h[0][dest, n % 4] if n < 4 else
                         yi_q[(n - 4) // 2][dest, n % 2])
                        .rearrange("b p d -> (b p) d"),
                        lambda dd=dd, buf=gi % 2:
                        ystg[buf][dd * 64:(dd + 1) * 64, :],
                        waits=([("ve", ev_val)] if dd == 0 else []))
                y_wr[gi] = (ysem, v)
            ym_te_end[n] = snap("te")
            if n + 4 < NE:
                for kc in range(DCH):
                    emit_wchunk(n + 4, kc)
            if n == 3:
                ym_syw = snap("syw")
                ym_syv_a = snap("syv")

                def f_a2a_ya(g, ym_syw=ym_syw, ym_syv_a=ym_syv_a):
                    g.wait_ge(sems["syw"], ym_syw)
                    g.wait_ge(sems["syv"], ym_syv_a)
                    with allow_ici_all_to_all():
                        g.collective_compute(
                            "AllToAll", OP.bypass,
                            replica_groups=[list(range(NCORE))],
                            ins=[yi_h[0].opt()], outs=[yo_h[0].opt()],
                        ).then_inc(sems["cc"], 1)
                emit("gpsimd", f_a2a_ya)
                a2aya_done = inc("cc", 1)
            if n == 5:
                yq1_syx = snap("syx")
                yq1_sy2 = snap("sy2")

                def f_a2a_yq0(g, a=yq1_syx, c=yq1_sy2):
                    g.wait_ge(sems["syx"], a)
                    g.wait_ge(sems["sy2"], c)
                    with allow_ici_all_to_all():
                        g.collective_compute(
                            "AllToAll", OP.bypass,
                            replica_groups=[list(range(NCORE))],
                            ins=[yi_q[0].opt()], outs=[yo_q[0].opt()],
                        ).then_inc(sems["cc"], 1)
                emit("gpsimd", f_a2a_yq0)
                a2ayq0_done = inc("cc", 1)
            te_now = snap("te")
            if n + 2 < NE:
                bv_d[n + 2] = dma("gpsimd", gpp[n % 2],
                                  lambda n=n: bexpr[0:1, (n % 2) * D:(n % 2 + 1) * D],
                                  lambda n=n: bexp_d[n + 2:n + 3, :],
                                  waits=[("te", te_now)])
        te_after_ym = snap("te")

        ysb0_ev = None
        for k in range(0, MCH, 2):
            w = [("cc", a2aya_done), ("te", ym_te_end[5])] if k == 0 else []
            ysb0_ev = dma("gpsimd", "gwc",
                          lambda k=k: ysb0[:, k, :],
                          lambda k=k: yo_h[0][k // 2, :, 0, :, :],
                          waits=w)
        for k in range(1, MCH, 2):
            w = [("cc", a2ayq0_done)] if k == 1 else []
            ysb0_ev = dma("gpsimd", "gwc",
                          lambda k=k: ysb0[0:64, k, :],
                          lambda k=k: yo_q[0][k // 2, :, 0, :, :],
                          waits=w)

        yq2_sy4 = snap("sy4")
        yq2_sy3 = snap("sy3")

        def f_a2a_yq1(g, a=yq2_sy4, c=yq2_sy3):
            g.wait_ge(sems["sy4"], a)
            g.wait_ge(sems["sy3"], c)
            with allow_ici_all_to_all():
                g.collective_compute(
                    "AllToAll", OP.bypass, replica_groups=[list(range(NCORE))],
                    ins=[yi_q[1].opt()], outs=[yo_q[1].opt()],
                ).then_inc(sems["cc"], 1)
        emit("gpsimd", f_a2a_yq1)
        a2ay_done = inc("cc", 1)

        shtp = ["shta", "shtb"]
        sop = ["soa", "sob"]
        ysb_b = [ysb0, ysb1]
        out_wr, a1_of, a2_of = {}, {}, {}
        yb0_last = None
        for k in range(1, MCH, 2):
            w = [("cc", a2ay_done)] if k == 1 else []
            yb0_last = dma("sync", "spk",
                           lambda k=k: ysb0[64:128, k, :],
                           lambda k=k: yo_q[1][k // 2, :, 0, :, :],
                           waits=w)
        yb1_last = None
        for k in range(MCH):
            if k % 2 == 0:
                w = [("cc", a2aya_done), ("te", te_after_ym)] if k == 0 else []
                yb1_last = dma("sync", "s6",
                               lambda k=k: ysb1[:, k, :],
                               lambda k=k: yo_h[0][k // 2, :, 1, :, :],
                               waits=w)
            else:
                w = [("cc", a2ayq0_done)] if k == 1 else []
                yb1_last = dma("sync", "s6",
                               lambda k=k: ysb1[0:64, k, :],
                               lambda k=k: yo_q[0][k // 2, :, 1, :, :],
                               waits=w)
        for k in range(1, MCH, 2):
            w = [("cc", a2ay_done)] if k == 1 else []
            yb1_last = dma("sync", "s6",
                           lambda k=k: ysb1[64:128, k, :],
                           lambda k=k: yo_q[1][k // 2, :, 1, :, :],
                           waits=w)

        eth_sem = {(0, 0): "str", (0, 1): "sp", (1, 0): "s5b", (1, 1): "str"}

        def emit_eth(b, hf, first_waits):
            last = None
            for k in range(MCH):
                last = dma("scalar", eth_sem[(b, hf)],
                           lambda b=b, hf=hf, k=k: eth[:, (2 * b + hf) % 2, k, :],
                           lambda b=b, hf=hf, k=k:
                           e_d[b, hf * 1024:(hf + 1) * 1024,
                               k * 128:(k + 1) * 128],
                           waits=(first_waits if k == 0 else []), transpose=True)
            return last

        eth_of = {(0, 0): emit_eth(0, 0, [("te", te_after_ym)]),
                  (0, 1): emit_eth(0, 1, [])}
        ent_w = {(0, 0): [(eth_sem[(0, 0)], eth_of[(0, 0)]), ("gwc", ysb0_ev),
                          ("spk", yb0_last)],
                 (0, 1): [(eth_sem[(0, 1)], eth_of[(0, 1)])]}
        for b in range(B):
            gi4 = 0
            for mc in range(MCH):
                i8 = b * MCH + mc
                pb = mc % 2
                xv = dma("sync", sxp[pb], lambda pb=pb: xt[:, pb, :],
                         lambda b=b, mc=mc: x_d[b, mc * 128:(mc + 1) * 128, :],
                         waits=[("ve", a1_of[i8 - 2]) if i8 >= 2 else
                                ("ve", snap("ve"))])
                hv = dma("scalar", shtp[pb], lambda pb=pb: htt[:, pb, :],
                         lambda b=b, mc=mc: h_d[b, :, mc * 128:(mc + 1) * 128],
                         waits=[("sha", p3_sha), ("shb", p3_shb),
                                ("ve", a2_of[i8 - 2]) if i8 >= 2 else None],
                         transpose=True)
                ob = pb * 2
                prev_out = out_wr.get(i8 - 2)
                for et in range(2):
                    bank = 4 + (gi4 % 4)
                    gi4 += 1
                    hb = (2 * b + mc // 8) % 2
                    mms = [(lambda kc=kc, mc=mc, hb=hb:
                            eth[:, hb, kc, (mc % 8) * 128:(mc % 8 + 1) * 128],
                            lambda kc=kc, et=et, b=b:
                            ysb_b[b][:, kc, et * 512:(et + 1) * 512])
                           for kc in range(MCH)]
                    if mc % 8 == 0 and et == 0:
                        barrier("tensor", ent_w[(b, mc // 8)])
                    mm_group(bank, mms)
                    consume(bank, "vector",
                            lambda e, bank=bank, b=b, mc=mc, et=et, ob=ob:
                            e.tensor_scalar(obuf[:, ob, et * 512:(et + 1) * 512],
                                            psa[bank][:, :],
                                            dcr[:, b * 16 + mc:b * 16 + mc + 1],
                                            None, OP.mult),
                            "ve",
                            waits=([(sop[pb], prev_out)] if (et == 0 and prev_out)
                                   else []))
                a1_of[i8] = ew("vector", "ve",
                               lambda e, ob=ob, pb=pb: e.tensor_add(
                                   obuf[:, ob + 1, :], obuf[:, ob, :], xt[:, pb, :]),
                               waits=[(sxp[pb], xv)])
                a2_of[i8] = ew("vector", "ve",
                               lambda e, ob=ob, pb=pb: e.tensor_add(
                                   obuf[:, ob, :], obuf[:, ob + 1, :], htt[:, pb, :]),
                               waits=[(shtp[pb], hv)])
                out_wr[i8] = dma("sync", sop[pb],
                                 lambda b=b, mc=mc:
                                 out_d[b, mc * 128:(mc + 1) * 128, :],
                                 lambda ob=ob: obuf[:, ob, :],
                                 waits=[("ve", a2_of[i8])])
                if b == 0 and mc == 9:
                    eth_of[(1, 0)] = emit_eth(1, 0, [("te", snap("te"))])
                    ent_w[(1, 0)] = [(eth_sem[(1, 0)], eth_of[(1, 0)]),
                                     ("s6", yb1_last)]
                if b == 0 and mc == 15:
                    eth_of[(1, 1)] = emit_eth(1, 1, [("te", snap("te"))])
                    ent_w[(1, 1)] = [(eth_sem[(1, 1)], eth_of[(1, 1)])]

        barrier("sync", [("soa", cnt["soa"]), ("sob", cnt["sob"])])

        @block.sync
        def _(sync):
            for f in plan["sync"]:
                f(sync)

        @block.scalar
        def _(scalar):
            for f in plan["scalar"]:
                f(scalar)

        @block.vector
        def _(vector):
            for f in plan["vector"]:
                f(vector)

        @block.tensor
        def _(tensor):
            for f in plan["tensor"]:
                f(tensor)

        @block.gpsimd
        def _(gpsimd):
            for f in plan["gpsimd"]:
                f(gpsimd)

    return nc


_NC_CACHE = []


def kernel(L0_attn_x, L0_attn_W_in, L0_attn_b_in, L0_attn_W_gate, L0_attn_b_gate,
           L0_attn_forget_lambda, L0_ffn_phi, L0_ffn_W, L0_ffn_b):
    x = np.ascontiguousarray(np.asarray(L0_attn_x, dtype=np.float32))
    W_in = np.asarray(L0_attn_W_in, dtype=np.float32)
    W_gate = np.asarray(L0_attn_W_gate, dtype=np.float32)
    b_in = np.asarray(L0_attn_b_in, dtype=np.float32)
    b_gate = np.asarray(L0_attn_b_gate, dtype=np.float32)
    lam = np.asarray(L0_attn_forget_lambda, dtype=np.float32)
    phi = np.ascontiguousarray(np.asarray(L0_ffn_phi, dtype=np.float32).reshape(D, NP))
    W = np.ascontiguousarray(np.asarray(L0_ffn_W, dtype=np.float32))
    bb = np.ascontiguousarray(np.asarray(L0_ffn_b, dtype=np.float32))

    wint = np.ascontiguousarray(W_in.T)
    wgt = np.ascontiguousarray(W_gate.T)
    binh = np.ascontiguousarray(b_in.reshape(DCH, 128).T)
    bgth = np.ascontiguousarray(b_gate.reshape(DCH, 128).T)
    lamh = np.ascontiguousarray(lam.reshape(DCH, 128).T)

    if not _NC_CACHE:
        _NC_CACHE.append(build())
    nc = _NC_CACHE[0]

    in_maps = []
    for c in range(NCORE):
        in_maps.append({
            "x": np.ascontiguousarray(x[2 * c:2 * c + 2]),
            "wint": wint,
            "wgt": wgt,
            "phi": phi,
            "wexp": np.ascontiguousarray(W[NE * c:NE * (c + 1)]),
            "binh": binh,
            "bgth": bgth,
            "lamh": lamh,
            "bexp": np.ascontiguousarray(bb[NE * c:NE * (c + 1)]),
        })
    res = run_bass_kernel_spmd(nc, in_maps, core_ids=list(range(NCORE)))
    out = np.concatenate([res.results[c]["out"] for c in range(NCORE)], axis=0)
    return out


# version 23 (final)
# speedup vs baseline: 1.0388x; 1.0388x over previous
import contextlib
import numpy as np
import concourse.bass as bass
import concourse.mybir as mybir
from concourse.bass_utils import run_bass_kernel_spmd
from concourse.aot_env import allow_ici_all_to_all

F32 = mybir.dt.float32
BF16 = mybir.dt.bfloat16
AF = mybir.ActivationFunctionType
OP = mybir.AluOpType
AX = mybir.AxisListType.X

NCORE = 8
B = 2
T = 2048
D = 1024
NE = 8
NPP = 32
NP = 2048
DCH = 8
MCH = 16
NPT = 4
EPS_LN = 1e-5
EPS_B = 1e-6

ENGS = ("sync", "scalar", "vector", "tensor", "gpsimd")
SEM_NAMES = ["sxa", "sxb", "sxn", "sev", "sp", "sha", "shb", "s5", "s6",
             "spk", "str", "shta", "shtb", "syw", "soa", "sob",
             "s5b", "s5c", "s5d", "sevb", "syx", "syv", "sy2", "sy3", "sy4",
             "gpa", "gpb", "gwc", "gwd", "cc", "te", "ve", "se", "fnc", "wrm"]


def build():
    nc = bass.Bass("TRN2", target_bir_lowering=False, debug=False,
                   detect_race_conditions=False)

    x_d = nc.dram_tensor("x", [B, T, D], F32, kind="ExternalInput").ap()
    wint_d = nc.dram_tensor("wint", [D, D], F32, kind="ExternalInput").ap()
    wgt_d = nc.dram_tensor("wgt", [D, D], F32, kind="ExternalInput").ap()
    phi_d = nc.dram_tensor("phi", [D, NP], F32, kind="ExternalInput").ap()
    wexp_d = nc.dram_tensor("wexp", [NE, D, D], F32, kind="ExternalInput").ap()
    bin_d = nc.dram_tensor("binh", [128, DCH], F32, kind="ExternalInput").ap()
    bgt_d = nc.dram_tensor("bgth", [128, DCH], F32, kind="ExternalInput").ap()
    lam_d = nc.dram_tensor("lamh", [128, DCH], F32, kind="ExternalInput").ap()
    bexp_d = nc.dram_tensor("bexp", [NE, D], F32, kind="ExternalInput").ap()
    out_d = nc.dram_tensor("out", [B, T, D], F32, kind="ExternalOutput").ap()

    xn_d = nc.dram_tensor("xn_s", [B, T, D], BF16).ap()
    h_d = nc.dram_tensor("h_s", [B, D, T], BF16).ap()
    e_d = nc.dram_tensor("e_s", [B, T, NP], BF16).ap()
    d1_d = nc.dram_tensor("d1_s", [T, NP], BF16).ap()
    p_d = nc.dram_tensor("p_s", [T, NP], F32).ap()
    dn_d = nc.dram_tensor("dn_s", [T, NP], F32).ap()
    sli_b = [nc.dram_tensor(f"sli{b}_s", [NCORE, DCH, 128, NE, NPP], BF16).ap()
             for b in range(B)]
    slo_b = [nc.dram_tensor(f"slo{b}_s", [NCORE, DCH, 128, NE, NPP], BF16).ap()
             for b in range(B)]
    yi_h = [nc.dram_tensor(f"yi{h}_s", [NCORE, 4, B, NPP, D], BF16).ap()
            for h in range(2)]
    yo_h = [nc.dram_tensor(f"yo{h}_s", [NCORE, 4, B, NPP, D], BF16).ap()
            for h in range(2)]
    yi_q = [nc.dram_tensor(f"yiq{h}_s", [NCORE, 2, B, NPP, D], BF16).ap()
            for h in range(2)]
    yo_q = [nc.dram_tensor(f"yoq{h}_s", [NCORE, 2, B, NPP, D], BF16).ap()
            for h in range(2)]

    A = nc.alloc_sbuf_tensor("arena_a", [128, 32768], BF16)
    xnfm = A.ap().rearrange("q (b c t) -> q b c t", b=B, c=DCH)
    dsb = A.ap().rearrange("q (m n) -> q m n", m=MCH)
    packed = A.ap().rearrange("q (c w) -> q c w", c=DCH)
    packed6 = A.ap().rearrange("q (c n s b p) -> q c n s b p", c=DCH, n=NE, s=NCORE, b=B)
    etsb = A.ap().rearrange("q (k t) -> q k t", k=MCH)
    Br = nc.alloc_sbuf_tensor("arena_b", [128, 16384], BF16)
    phisb = Br.ap().rearrange("q (c n) -> q c n", c=DCH)
    xntok = Br.ap().rearrange("q (m d) -> q m d", m=MCH)
    wbuf = Br.ap().rearrange("q (w c e) -> q w c e", w=2, c=DCH)
    ysb = Br.ap().rearrange("q (k e) -> q k e", k=MCH)
    C = nc.alloc_sbuf_tensor("arena_c", [128, 16384], BF16)
    cc2 = C.ap().rearrange("q (h c n) -> q h c n", h=2, c=DCH)
    wisb = cc2[:, 0]
    wgsb = cc2[:, 1]

    est = nc.alloc_sbuf_tensor("est", [128, B, 2, 512], BF16).ap()
    hstage = nc.alloc_sbuf_tensor("hstage", [128, 2, T], BF16).ap()
    htt = nc.alloc_sbuf_tensor("htt", [128, 2, D], BF16).ap()
    slstage = nc.alloc_sbuf_tensor("slstage", [128, DCH, 512], BF16).ap()
    ystage = slstage.rearrange("q c n -> q (c n)")[:, 0:D]
    d1st = hstage
    onesb = nc.alloc_sbuf_tensor("onesb", [1, 128], BF16).ap()
    bexpr = nc.alloc_sbuf_tensor("bexpr", [1, 2 * D], BF16).ap()

    R = nc.alloc_sbuf_tensor("arena_r", [128, 5 * T], F32)
    rows = [R.ap()[:, i * T:(i + 1) * T] for i in range(5)]
    dnrow = R.ap()[:, 0:NP]
    rrow = R.ap()[:, NP:2 * NP]
    obuf = R.ap().rearrange("q (k t) -> q k t", k=5)[:, 0:4, 0:D]

    xt = R.ap().rearrange("q (k t) -> q k t", k=5)[:, 0:4, D:2 * D]
    pst = nc.alloc_sbuf_tensor("pst", [128, 512], F32).ap()
    accs = nc.alloc_sbuf_tensor("accs", [128, 128], F32).ap()
    dcs = nc.alloc_sbuf_tensor("dcs", [128, 32], F32).ap()
    dcr = nc.alloc_sbuf_tensor("dcr", [128, 32], F32).ap()
    stat = nc.alloc_sbuf_tensor("stat", [128, 64], F32).ap()
    smls = nc.alloc_sbuf_tensor("smls", [128, 4 * DCH], F32).ap()
    s_half = smls[:, 0:DCH]
    bih = smls[:, DCH:2 * DCH]
    bgh = smls[:, 2 * DCH:3 * DCH]
    stmp = smls[:, 3 * DCH:4 * DCH]
    binsb = nc.alloc_sbuf_tensor("binsb", [128, DCH], F32).ap()
    bgtsb = nc.alloc_sbuf_tensor("bgtsb", [128, DCH], F32).ap()
    lamsb = nc.alloc_sbuf_tensor("lamsb", [128, DCH], F32).ap()

    psum = [nc.alloc_psum_tensor(f"ps{i}", [128, 512], F32) for i in range(8)]
    psa = [p.ap() for p in psum]

    plan = {e: [] for e in ENGS}
    sems = {}
    cnt = {s: 0 for s in SEM_NAMES}

    def emit(eng, fn):
        plan[eng].append(fn)

    def inc(s, n):
        cnt[s] += n
        return cnt[s]

    def snap(s):
        return cnt[s]

    def barrier(eng, waits):
        waits = tuple(w for w in waits if w is not None)

        def f(e, waits=waits):
            for (s, v) in waits:
                e.wait_ge(sems[s], v)
        emit(eng, f)

    bank_prod = [None] * 8
    bank_cons = [None] * 8

    def mm_group(bank, mms):
        wait = bank_cons[bank]

        def f(t, wait=wait, mms=mms, bank=bank):
            if wait is not None:
                t.wait_ge(sems[wait[0]], wait[1])
            n = len(mms)
            for i, (lf, rf) in enumerate(mms):
                ins = t.matmul(psa[bank][:, :], lf(), rf(),
                               start=(i == 0), stop=(i == n - 1))
            ins.then_inc(sems["te"], 1)
        emit("tensor", f)
        bank_prod[bank] = ("te", inc("te", 1))

    def mm_group_multi(banks_mms):
        waits = [bank_cons[b] for b, _ in banks_mms]

        def f(t, waits=tuple(waits), banks_mms=banks_mms):
            for w in waits:
                if w is not None:
                    t.wait_ge(sems[w[0]], w[1])
            nk = len(banks_mms[0][1])
            last = None
            for i in range(nk):
                for b, mms in banks_mms:
                    lf, rf = mms[i]
                    last = t.matmul(psa[b][:, :], lf(), rf(),
                                    start=(i == 0), stop=(i == nk - 1))
            last.then_inc(sems["te"], 1)
        emit("tensor", f)
        v = inc("te", 1)
        for b, _ in banks_mms:
            bank_prod[b] = ("te", v)

    def consume(bank, eng, fn, sem, waits=()):
        prod = bank_prod[bank]

        def f(e, prod=prod, fn=fn, waits=tuple(w for w in waits if w is not None)):
            for (s, v) in waits:
                e.wait_ge(sems[s], v)
            if prod is not None:
                e.wait_ge(sems[prod[0]], prod[1])
            fn(e).then_inc(sems[sem], 1)
        emit(eng, f)
        bank_cons[bank] = (sem, inc(sem, 1))

    def dma(eng, sem, out_fn, in_fn, waits=(), transpose=False):
        def f(e, waits=tuple(w for w in waits if w is not None),
              transpose=transpose):
            for (s, v) in waits:
                e.wait_ge(sems[s], v)
            if transpose:
                e.dma_start_transpose(out_fn(), in_fn()).then_inc(sems[sem], 16)
            else:
                e.dma_start(out=out_fn(), in_=in_fn()).then_inc(sems[sem], 16)
        emit(eng, f)
        return inc(sem, 16)

    def spacer(eng):
        emit(eng, lambda e: e.drain())

    def ew(eng, sem, fn, waits=()):
        def f(e, waits=tuple(w for w in waits if w is not None), fn=fn):
            for (s, v) in waits:
                e.wait_ge(sems[s], v)
            fn(e).then_inc(sems[sem], 1)
        emit(eng, f)
        return inc(sem, 1)

    with contextlib.ExitStack() as stack:
        for s in SEM_NAMES:
            sems[s] = stack.enter_context(nc.semaphore(s))
        ids = [sems[s].num for s in SEM_NAMES]
        srange = range(min(ids), max(ids) + 1)
        nc.gpsimd.dma_reset(srange)
        nc.gpsimd.sem_clear(srange)
        nc.all_engine_barrier()
        block = stack.enter_context(nc.Block())

        def fwarm_gp(e):
            for k in range(4):
                e.dma_start(out=lamsb[:, :], in_=lam_d[:, :]).then_inc(
                    sems["fnc"], 16)
                e.wait_ge(sems["fnc"], 16 * (k + 1))
            e.sem_inc(sems["wrm"], 1)
        emit("gpsimd", fwarm_gp)
        cnt["fnc"] += 64
        cnt["wrm"] += 1
        for eng in ("sync", "scalar", "vector", "tensor"):
            def fwarm(e):
                e.wait_ge(sems["wrm"], 1)
            emit(eng, fwarm)

        dma("sync", "s5", lambda: binsb[:, :], lambda: bin_d[:, :])
        dma("sync", "s5", lambda: bgtsb[:, :], lambda: bgt_d[:, :])
        v_small = dma("sync", "s5", lambda: lamsb[:, :], lambda: lam_d[:, :])
        v_phi = dma("gpsimd", "gpa", lambda: phisb[:, :, :],
                    lambda: phi_d.rearrange("(c p) n -> p c n", p=128))
        v_wi = dma("gpsimd", "gpa", lambda: wisb[:, :, :],
                   lambda: wint_d.rearrange("(c p) n -> p c n", p=128))
        v_wg = dma("gpsimd", "gpa", lambda: wgsb[:, :, :],
                   lambda: wgt_d.rearrange("(c p) n -> p c n", p=128))
        ew("scalar", "se", lambda e: e.activation(stmp[:, :], lamsb[:, :], AF.Exp),
           waits=[("s5", v_small)])
        spacer("scalar")
        ew("scalar", "se", lambda e: e.add(stmp[:, :], stmp[:, :], 1.0))
        spacer("scalar")
        ew("scalar", "se", lambda e: e.activation(stmp[:, :], stmp[:, :], AF.Ln))
        spacer("scalar")
        p0_se = ew("scalar", "se", lambda e: e.mul(s_half[:, :], stmp[:, :], -4.0))
        ew("vector", "ve", lambda e: e.tensor_scalar_mul(bih[:, :], binsb[:, :], 0.5),
           waits=[("s5", v_small)])
        ew("vector", "ve", lambda e: e.tensor_scalar_mul(bgh[:, :], bgtsb[:, :], 0.5))
        ew("vector", "ve", lambda e: e.memset(onesb[:, :], 1.0))

        xloadp, xapply, sq_of, sd_of, xn_wrp = {}, {}, {}, {}, {}
        sxp = ["sxa", "sxb"]
        p1_b_done = {}

        def ln_chunk(i, slot, b, mc):
            ss = stat[:, 8 * (i % 4):8 * (i % 4) + 8]
            pr = i // 2
            wr = [("ve", xapply[i - 4]), ("se", sd_of[i - 4])] if i >= 4 else []
            ew("vector", "ve",
               lambda e, slot=slot, ss=ss: e.reduce_sum(ss[:, 0:1], xt[:, slot, :],
                                                        axis=AX),
               waits=wr + [(sxp[pr % 2], xloadp[pr])])
            sq_of[i] = ew("scalar", "se",
                          lambda e, slot=slot, ss=ss: e.activation(
                              hstage[:, 0, 0:D], xt[:, slot, :], AF.Square,
                              accum_out=ss[:, 1:2]),
                          waits=([("se", sd_of[i - 4])] if i >= 4 else []) +
                          [(sxp[pr % 2], xloadp[pr])])
            spacer("vector")
            ew("vector", "ve",
               lambda e, ss=ss: e.tensor_scalar_mul(ss[:, 2:3], ss[:, 0:1], 1.0 / D),
               waits=[("se", sq_of[i])])
            spacer("vector")
            ew("vector", "ve",
               lambda e, ss=ss: e.tensor_mul(ss[:, 3:4], ss[:, 2:3], ss[:, 2:3]))
            ew("vector", "ve",
               lambda e, ss=ss: e.tensor_scalar(ss[:, 4:5], ss[:, 1:2], 1.0 / D,
                                                float(EPS_LN), OP.mult, OP.add))
            spacer("vector")
            ew("vector", "ve",
               lambda e, ss=ss: e.tensor_sub(ss[:, 5:6], ss[:, 4:5], ss[:, 3:4]))
            sd_of[i] = ew("scalar", "se",
                          lambda e, ss=ss: e.activation(ss[:, 6:7], ss[:, 5:6],
                                                        AF.Sqrt),
                          waits=[("ve", snap("ve"))])
            ew("vector", "ve",
               lambda e, ss=ss: e.reciprocal(ss[:, 7:8], ss[:, 6:7]),
               waits=[("se", sd_of[i])])
            spacer("vector")
            k = i % 2
            wa = [("sxn", xn_wrp[pr - 1])] if pr >= 1 else []
            xapply[i] = ew("vector", "ve",
                           lambda e, slot=slot, ss=ss, k=k: e.tensor_scalar(
                               hstage[:, 1, k * D:(k + 1) * D], xt[:, slot, :],
                               ss[:, 2:3], ss[:, 7:8],
                               OP.subtract, OP.mult),
                           waits=wa)

        def ln_pair(pr, b, pc):
            w = []
            if pr >= 2:
                w = [("ve", xapply[2 * (pr - 2) + 1]), ("se", sq_of[2 * (pr - 2) + 1])]
            xloadp[pr] = dma(
                "sync", sxp[pr % 2],
                lambda pr=pr: xt[:, 2 * (pr % 2):2 * (pr % 2) + 2, :],
                lambda b=b, pc=pc: x_d[b, pc * 256:(pc + 1) * 256, :].rearrange(
                    "(a q) d -> q a d", q=128),
                waits=w)
            ln_chunk(2 * pr, 2 * (pr % 2), b, 2 * pc)
            ln_chunk(2 * pr + 1, 2 * (pr % 2) + 1, b, 2 * pc + 1)
            xn_wrp[pr] = dma(
                "sync", "sxn",
                lambda b=b, pc=pc: xn_d[b, pc * 256:(pc + 1) * 256, :].rearrange(
                    "(a q) d -> q a d", q=128),
                lambda: hstage[:, 1, 0:2 * D].rearrange("q (a d) -> q a d", a=2),
                waits=[("ve", xapply[2 * pr + 1])])

        pr = 0
        for b in range(B):
            for pc in range(8):
                ln_pair(pr, b, pc)
                pr += 1
            if b == 0:
                ln_pair(pr, 0, 0)
                pr += 1
                p1_b_done[0] = snap("sxn")
                for c in range(DCH):
                    dma("scalar", "str",
                        lambda c=c: xnfm[:, 0, c, :],
                        lambda c=c: xn_d[0, :, c * 128:(c + 1) * 128],
                        waits=([("sxn", p1_b_done[0])] if c == 0 else []),
                        transpose=True)
        p1_sxn = snap("sxn")

        for c in range(DCH):
            dma("scalar", "str",
                lambda c=c: xnfm[:, 1, c, :],
                lambda c=c: xn_d[1, :, c * 128:(c + 1) * 128],
                waits=([("sxn", p1_sxn)] if c == 0 else []),
                transpose=True)
        p2_done = snap("str")

        barrier("tensor", [("gpa", v_wg), ("str", p2_done)])
        gi = 0
        prev_p_wr = {}
        est_free = {}
        for mc in range(MCH):
            for npt in range(NPT):
                par = npt % 2
                for b in range(B):
                    bank = gi % 4
                    gi += 1
                    mms = []
                    for kc in range(DCH):
                        mms.append((
                            lambda b=b, kc=kc, mc=mc:
                            xnfm[:, b, kc, mc * 128:(mc + 1) * 128],
                            lambda kc=kc, npt=npt:
                            phisb[:, kc, npt * 512:(npt + 1) * 512],
                        ))
                    mm_group(bank, mms)
                    col = b * 64 + mc * 4 + npt
                    if b == 0:
                        eng_e, sevp = "sync", ("sev" if par == 0 else "sevb")
                    else:
                        eng_e, sevp = "gpsimd", ("gpa" if par == 0 else "gpb")
                    consume(bank, "scalar",
                            lambda e, b=b, col=col, bank=bank, par=par:
                            e.activation(est[:, b, par, :], psa[bank][:, :], AF.Exp,
                                         accum_out=accs[:, col:col + 1]),
                            "se", waits=list(est_free.get((b, par), ())))
                    dma(eng_e, sevp,
                        lambda b=b, mc=mc, npt=npt:
                        e_d[b, mc * 128:(mc + 1) * 128,
                            npt * 512:(npt + 1) * 512],
                        lambda b=b, par=par: est[:, b, par, :],
                        waits=[("se", snap("se"))])
                pv = ew("vector", "ve",
                        lambda e, par=par: e.tensor_add(pst[:, :], est[:, 0, par, :],
                                                        est[:, 1, par, :]),
                        waits=[("se", snap("se")),
                               ("sp", prev_p_wr.get(0)) if prev_p_wr.get(0) else None])
                prev_p_wr[0] = dma("scalar", "sp",
                                   lambda mc=mc, npt=npt:
                                   p_d[mc * 128:(mc + 1) * 128,
                                       npt * 512:(npt + 1) * 512],
                                   lambda: pst[:, :], waits=[("ve", pv)])
                for b in range(B):
                    sv = ("sev" if par == 0 else "sevb") if b == 0 else \
                         ("gpa" if par == 0 else "gpb")
                    est_free[(b, par)] = [(sv, snap(sv)), ("ve", pv)]
        p4_sev = snap("sev")
        p4_sevb = snap("sevb")
        p4_gpa = snap("gpa")
        p4_gpb = snap("gpb")
        for b in range(B):
            for mc in range(MCH):
                col = b * 64 + mc * 4
                ew("vector", "ve",
                   lambda e, col=col, b=b, mc=mc: e.reduce_sum(
                       dcs[:, b * 16 + mc:b * 16 + mc + 1], accs[:, col:col + 4],
                       axis=AX),
                   waits=([("se", snap("se"))] if (b == 0 and mc == 0) else []))
        spacer("vector")
        ew("vector", "ve", lambda e: e.reciprocal(dcr[:, :], dcs[:, :]))

        p4_sp = snap("sp")

        def f_ar(g, p4_sp=p4_sp):
            g.wait_ge(sems["sp"], p4_sp)
            g.collective_compute(
                "AllReduce", OP.add, replica_groups=[list(range(NCORE))],
                ins=[p_d.opt()], outs=[dn_d.opt()],
            ).then_inc(sems["cc"], 1)
        emit("gpsimd", f_ar)
        ar_done = inc("cc", 1)

        xn0_last = None
        for mc in range(MCH):
            xn0_last = dma("sync", "s6", lambda mc=mc: xntok[:, mc, :],
                           lambda mc=mc: xn_d[0, mc * 128:(mc + 1) * 128, :],
                           waits=([("te", snap("te"))] if mc == 0 else []))

        barrier("tensor", [("gpa", v_wg)])
        barrier("scalar", [("ve", 3), ("se", p0_se)])
        pair = 0
        h_wr = {}
        shp = ["sha", "shb"]
        prev_t2 = None
        scan_of = {}
        for b in range(B):
            for dc in range(DCH):
                ci = b * DCH + dc
                if prev_t2 is not None:
                    barrier("scalar", [("ve", prev_t2)])
                tanh_done = []
                for mt in range(4):
                    bg = 2 + (pair % 2) * 2
                    bi_ = bg + 1
                    pair += 1
                    mmg = [(lambda kc=kc, dc=dc: wgsb[:, kc, dc * 128:(dc + 1) * 128],
                            lambda b=b, kc=kc, mt=mt:
                            xnfm[:, b, kc, mt * 512:(mt + 1) * 512])
                           for kc in range(DCH)]
                    mmi = [(lambda kc=kc, dc=dc: wisb[:, kc, dc * 128:(dc + 1) * 128],
                            lambda b=b, kc=kc, mt=mt:
                            xnfm[:, b, kc, mt * 512:(mt + 1) * 512])
                           for kc in range(DCH)]
                    mm_group(bg, mmg)
                    mm_group(bi_, mmi)
                    consume(bg, "scalar",
                            lambda e, bg=bg, mt=mt, dc=dc: e.activation(
                                rows[0][:, mt * 512:(mt + 1) * 512], psa[bg][:, :],
                                AF.Tanh, scale=0.5, bias=bgh[:, dc:dc + 1]),
                            "se")
                    consume(bi_, "scalar",
                            lambda e, bi_=bi_, mt=mt, dc=dc: e.activation(
                                rows[1][:, mt * 512:(mt + 1) * 512], psa[bi_][:, :],
                                AF.Tanh, scale=0.5, bias=bih[:, dc:dc + 1]),
                            "se")
                    tanh_done.append(snap("se"))
                for hf in range(2):
                    sl = slice(hf * 1024, (hf + 1) * 1024)
                    av = ew("scalar", "se",
                            lambda e, sl=sl, dc=dc: e.activation(
                                rows[2][:, sl], rows[0][:, sl], AF.Exp,
                                scale=s_half[:, dc:dc + 1], bias=s_half[:, dc:dc + 1]),
                            waits=[("se", tanh_done[2 * hf + 1])] +
                            ([("ve", scan_of[(ci - 1, hf)])] if ci >= 1 else []))
                    ew("vector", "ve",
                       lambda e, sl=sl: e.tensor_mul(rows[3][:, sl], rows[2][:, sl],
                                                     rows[2][:, sl]),
                       waits=[("se", av)])
                    ew("vector", "ve",
                       lambda e, sl=sl: e.tensor_scalar(rows[4][:, sl], rows[3][:, sl],
                                                        -1.0, 1.0 + EPS_B,
                                                        OP.mult, OP.add))
                bsq = []
                for hf in range(2):
                    sl = slice(hf * 1024, (hf + 1) * 1024)
                    bsq.append(ew("scalar", "se",
                                  lambda e, sl=sl: e.activation(
                                      rows[3][:, sl], rows[4][:, sl], AF.Sqrt),
                                  waits=[("ve", snap("ve"))]))
                for hf in range(2):
                    sl = slice(hf * 1024, (hf + 1) * 1024)
                    ew("vector", "ve",
                       lambda e, sl=sl: e.tensor_scalar(rows[4][:, sl], rows[1][:, sl],
                                                        0.5, 0.5, OP.mult, OP.add),
                       waits=[("se", bsq[hf])])
                    tv = ew("vector", "ve",
                            lambda e, sl=sl: e.tensor_mul(rows[1][:, sl],
                                                          rows[4][:, sl],
                                                          rows[3][:, sl]))
                    if hf == 1:
                        prev_t2 = tv
                for hf in range(2):
                    sl = slice(hf * 1024, (hf + 1) * 1024)
                    ew("vector", "ve",
                       lambda e, sl=sl, b=b, dc=dc: e.tensor_mul(
                           rows[4][:, sl], rows[1][:, sl], xnfm[:, b, dc, sl]))
                hp = ci % 2
                wv_ = [(shp[hp], h_wr[ci - 2])] if ci >= 2 else []
                scan_of[(ci, 0)] = ew(
                    "vector", "ve",
                    lambda e, hp=hp: e.tensor_tensor_scan(
                        hstage[:, hp, 0:1024], rows[2][:, 0:1024], rows[4][:, 0:1024],
                        0.0, OP.mult, OP.add),
                    waits=wv_)
                spacer("vector")
                scan_of[(ci, 1)] = ew(
                    "vector", "ve",
                    lambda e, hp=hp: e.tensor_tensor_scan(
                        hstage[:, hp, 1024:2048], rows[2][:, 1024:2048],
                        rows[4][:, 1024:2048],
                        hstage[:, hp, 1023:1024], OP.mult, OP.add))
                h_wr[ci] = dma("sync", shp[hp],
                               lambda b=b, dc=dc: h_d[b, dc * 128:(dc + 1) * 128, :],
                               lambda hp=hp: hstage[:, hp, :],
                               waits=[("ve", scan_of[(ci, 1)])])
        p3_ve = snap("ve")
        p3_sha, p3_shb = snap("sha"), snap("shb")

        dn_t = [rows[0], rows[1], rows[2], rows[3]]
        e0_t = [hstage[:, 0, :], hstage[:, 1, :]]
        slflat = slstage.rearrange("q c n -> q (c n)")
        e1_t = [slflat[:, 0:NP], slflat[:, NP:2 * NP]]
        s5p_n = ["s5", "s5b", "s5c", "s5d"]
        eld_n = ["sha", "shb"]
        rv_of, d1v_of, d1wr_of = {}, {}, {}

        def emit_d1wr_odd(md):
            d1wr_of[md] = ("sev", dma(
                "sync", "sev",
                lambda md=md: d1_d[md * 128:(md + 1) * 128, :],
                lambda md=md: e0_t[md % 2],
                waits=[("ve", d1v_of[md])]))

        for mc in range(MCH):
            p = mc % 2
            q4 = mc % 4
            if mc < 4:
                wl = [("cc", ar_done), ("ve", p3_ve),
                      ("sha", p3_sha), ("shb", p3_shb),
                      ("sev", p4_sev), ("sevb", p4_sevb)]
            else:
                wl = [("ve", d1v_of[mc - 4])]
            ldn = dma("sync", s5p_n[q4], lambda q4=q4: dn_t[q4][:, :],
                      lambda mc=mc: dn_d[mc * 128:(mc + 1) * 128, :], waits=wl)
            if mc % 2 == 1 and mc >= 3:
                emit_d1wr_odd(mc - 2)
            if mc < 2:
                wg = [("ve", p3_ve), ("sha", p3_sha), ("shb", p3_shb),
                      ("sev", p4_sev), ("sevb", p4_sevb),
                      ("gpa", p4_gpa), ("gpb", p4_gpb)]
            else:
                wg = [d1wr_of[mc - 2]]
            le0 = dma("scalar", eld_n[p], lambda p=p: e0_t[p],
                      lambda mc=mc: e_d[0, mc * 128:(mc + 1) * 128, :], waits=wg)
            le1 = dma("scalar", eld_n[p], lambda p=p: e1_t[p],
                      lambda mc=mc: e_d[1, mc * 128:(mc + 1) * 128, :])
            ew("scalar", "se",
               lambda e, q4=q4: e.activation(dn_t[q4][:, :], dn_t[q4][:, :], AF.Ln),
               waits=[(s5p_n[q4], ldn)])
            rv_of[mc] = ew("scalar", "se",
                           lambda e, q4=q4: e.activation(dn_t[q4][:, :],
                                                         dn_t[q4][:, :],
                                                         AF.Exp, scale=-1.0))
            ew("vector", "ve",
               lambda e, mc=mc, p=p, q4=q4: e.tensor_mul(dsb[:, mc, :], e0_t[p],
                                                         dn_t[q4][:, :]),
               waits=[("se", rv_of[mc]), (eld_n[p], le1)])
            d1v_of[mc] = ew("vector", "ve",
                            lambda e, p=p, q4=q4: e.tensor_mul(e0_t[p], e1_t[p],
                                                               dn_t[q4][:, :]))
            if mc % 2 == 0:
                d1wr_of[mc] = ("gwd", dma(
                    "gpsimd", "gwd",
                    lambda mc=mc: d1_d[mc * 128:(mc + 1) * 128, :],
                    lambda p=p: e0_t[p], waits=[("ve", d1v_of[mc])]))
        emit_d1wr_odd(15)
        p5_s5 = snap("s5")
        p5_ve = snap("ve")

        def emit_slots(batch, sname, first_waits):
            gi2 = 0
            stage_guard = None
            for npt in range(NPT):
                for dc in range(DCH):
                    bank = 6 + (gi2 % 2)
                    gi2 += 1
                    mms = [(lambda m=m, dc=dc: xntok[:, m, dc * 128:(dc + 1) * 128],
                            lambda m=m, npt=npt: dsb[:, m, npt * 512:(npt + 1) * 512])
                           for m in range(MCH)]
                    if npt == 0 and dc == 0:
                        barrier("tensor", first_waits)
                    mm_group(bank, mms)
                    consume(bank, "vector",
                            lambda e, bank=bank, dc=dc:
                            e.tensor_copy(slstage[:, dc, :], psa[bank][:, :]),
                            "ve",
                            waits=([stage_guard] if (stage_guard and dc == 0) else []))
                ev_last = snap("ve")
                last = None
                for j in range(2):
                    dest = 2 * npt + j
                    last = dma(
                        "sync", sname,
                        lambda dest=dest, batch=batch:
                        sli_b[batch][dest].rearrange("c q n p -> q c (n p)"),
                        lambda j=j: slstage[:, :, j * 256:(j + 1) * 256],
                        waits=[("ve", ev_last)])
                stage_guard = (sname, last)

        emit_slots(0, "s6", [("s6", xn0_last), ("ve", p5_ve)])
        s6_b0 = snap("s6")

        def f_a2a_s0(g):
            g.wait_ge(sems["s6"], s6_b0)
            with allow_ici_all_to_all():
                g.collective_compute(
                    "AllToAll", OP.bypass, replica_groups=[list(range(NCORE))],
                    ins=[sli_b[0].opt()], outs=[slo_b[0].opt()],
                ).then_inc(sems["cc"], 1)
        emit("gpsimd", f_a2a_s0)
        a2as0_done = inc("cc", 1)

        ld_w = [("te", snap("te")), ("gwd", snap("gwd")), ("sev", snap("sev"))]
        dma("sync", "spk",
            lambda: dsb[:, 0:8, :],
            lambda: d1_d.rearrange("(m q) n -> q m n", q=128)[:, 0:8, :],
            waits=ld_w)
        xn1_ld = dma("sync", "spk",
                     lambda: xntok[:, 0:8, :],
                     lambda: xn_d[1].rearrange("(m q) d -> q m d", q=128)[:, 0:8, :])
        dma("scalar", "shta",
            lambda: dsb[:, 8:16, :],
            lambda: d1_d.rearrange("(m q) n -> q m n", q=128)[:, 8:16, :],
            waits=ld_w)
        xn1_lds = dma("scalar", "shta",
                      lambda: xntok[:, 8:16, :],
                      lambda: xn_d[1].rearrange("(m q) d -> q m d", q=128)[:, 8:16, :])
        emit_slots(1, "spk", [("spk", xn1_ld), ("shta", xn1_lds)])
        spk_b1 = snap("spk")
        te_after_p6 = snap("te")

        def f_a2a_s1(g):
            g.wait_ge(sems["spk"], spk_b1)
            with allow_ici_all_to_all():
                g.collective_compute(
                    "AllToAll", OP.bypass, replica_groups=[list(range(NCORE))],
                    ins=[sli_b[1].opt()], outs=[slo_b[1].opt()],
                ).then_inc(sems["cc"], 1)
        emit("gpsimd", f_a2a_s1)
        a2as_done = inc("cc", 1)

        first_q = {}
        qi = 0
        pk_last = {}
        for bq in range(B):
            for s in range(NCORE):
                for c in range(DCH):
                    eng, sem = (("sync", "spk"), ("scalar", "str"),
                                ("gpsimd", "gwc"))[qi % 3]
                    qi += 1
                    w = []
                    if (eng, bq) not in first_q:
                        w = [("cc", a2as0_done if bq == 0 else a2as_done),
                             ("te", te_after_p6)]
                        first_q[(eng, bq)] = True
                    v = dma(eng, sem,
                            lambda s=s, c=c, bq=bq: packed6[:, c, :, s, bq, :],
                            lambda s=s, c=c, bq=bq: slo_b[bq][s, c, :, :, :],
                            waits=w)
                    pk_last[sem] = v
        pk_sy = pk_last.get("spk", 0)
        pk_tr = pk_last.get("str", 0)
        pk_gw = pk_last.get("gwc", 0)
        gpp = ["gpa", "gpb"]
        wslot = [wisb, wgsb, wbuf[:, 0], wbuf[:, 1]]
        wstage = [rows[k][:, 0:D] for k in range(4)]
        wssem = ["sxn", "shta", "shtb", "sob"]
        wseng = ["sync", "scalar", "sync", "scalar"]
        cast_of = {}
        wld_of = {}
        p5_rows_free = ("ve", snap("ve"))

        def emit_wchunk(n, kc):
            i = n * 8 + kc
            s4 = i % 4
            wld_of[i] = dma(
                wseng[s4], wssem[s4],
                lambda s4=s4: wstage[s4],
                lambda n=n, kc=kc: wexp_d[n, kc * 128:(kc + 1) * 128, :],
                waits=[("se", cast_of[i - 4]) if i >= 4 else p5_rows_free])
            if kc == 0:
                barrier("vector", [("te", ym_te_end.get(n - 4, te_after_p6))])
            cast_of[i] = ew("vector", "se",
                            lambda e, n=n, kc=kc, s4=s4: e.tensor_copy(
                                wslot[n % 4][:, kc, :], wstage[s4]),
                            waits=[(wssem[s4], wld_of[i])])

        ym_te_end = {}
        bv_d = {}
        for n in range(2):
            emit_wchunk(n, 0) if False else None
        for n in range(4):
            for kc in range(DCH):
                emit_wchunk(n, kc)
        for n in range(2):
            bv_d[n] = dma("gpsimd", gpp[n % 2],
                          lambda n=n: bexpr[0:1, (n % 2) * D:(n % 2 + 1) * D],
                          lambda n=n: bexp_d[n:n + 1, :])
        ysb0 = C.ap().rearrange("q (k e) -> q k e", k=MCH)
        ysb1 = ysb
        eth = A.ap().rearrange("q (h k t) -> q h k t", h=2, k=MCH)

        ystg = [ystage, slflat[:, D:2 * D]]
        y_wr = {}
        gi3 = 0
        for n in range(NE):
            barrier("tensor", [("se", cast_of[n * 8 + 7]), (gpp[n % 2], bv_d[n])] +
                    ([("spk", pk_sy), ("str", pk_tr), ("gwc", pk_gw)]
                     if n == 0 else []))
            for ic in range(4):
                gi = n * 4 + ic
                b0 = (gi3 % 2) * 2
                gi3 += 1
                pairs = []
                for j in range(2):
                    mms = [(lambda n=n, ic=ic, kc=kc:
                            packed[:, kc, n * 512 + ic * 128:n * 512 + (ic + 1) * 128],
                            lambda n=n, kc=kc, j=j:
                            wslot[n % 4][:, kc, j * 512:(j + 1) * 512])
                           for kc in range(DCH)]
                    mms.append((
                        lambda: onesb[0:1, :],
                        lambda n=n, j=j: bexpr[0:1, (n % 2) * D + j * 512:
                                               (n % 2) * D + (j + 1) * 512],
                    ))
                    pairs.append((b0 + j, mms))
                mm_group_multi(pairs)
                prev_wr = y_wr.get(gi - 2)
                for j in range(2):
                    consume(b0 + j, "vector",
                            lambda e, bk=b0 + j, j=j, buf=gi % 2:
                            e.tensor_copy(ystg[buf][:, j * 512:(j + 1) * 512],
                                          psa[bk][:, :]),
                            "ve",
                            waits=([prev_wr] if (j == 0 and prev_wr) else []))
                ev_val = snap("ve")
                if gi % 2 == 0:
                    yeng = "scalar"
                    ysem = "syv" if n < 4 else ("sy2" if n < 6 else "sy3")
                else:
                    yeng = "sync"
                    ysem = "syw" if n < 4 else ("syx" if n < 6 else "sy4")
                for dd in range(2):
                    dest = 2 * ic + dd
                    v = dma(
                        yeng, ysem,
                        lambda dest=dest, n=n:
                        (yi_h[0][dest, n % 4] if n < 4 else
                         yi_q[(n - 4) // 2][dest, n % 2])
                        .rearrange("b p d -> (b p) d"),
                        lambda dd=dd, buf=gi % 2:
                        ystg[buf][dd * 64:(dd + 1) * 64, :],
                        waits=([("ve", ev_val)] if dd == 0 else []))
                y_wr[gi] = (ysem, v)
                if ic == 0 and 1 <= n <= 4:
                    for kc in range(4, DCH):
                        emit_wchunk(n + 3, kc)
            ym_te_end[n] = snap("te")
            if n + 4 < NE:
                for kc in range(4):
                    emit_wchunk(n + 4, kc)
            if n == 3:
                ym_syw = snap("syw")
                ym_syv_a = snap("syv")

                def f_a2a_ya(g, ym_syw=ym_syw, ym_syv_a=ym_syv_a):
                    g.wait_ge(sems["syw"], ym_syw)
                    g.wait_ge(sems["syv"], ym_syv_a)
                    with allow_ici_all_to_all():
                        g.collective_compute(
                            "AllToAll", OP.bypass,
                            replica_groups=[list(range(NCORE))],
                            ins=[yi_h[0].opt()], outs=[yo_h[0].opt()],
                        ).then_inc(sems["cc"], 1)
                emit("gpsimd", f_a2a_ya)
                a2aya_done = inc("cc", 1)
            if n == 5:
                yq1_syx = snap("syx")
                yq1_sy2 = snap("sy2")

                def f_a2a_yq0(g, a=yq1_syx, c=yq1_sy2):
                    g.wait_ge(sems["syx"], a)
                    g.wait_ge(sems["sy2"], c)
                    with allow_ici_all_to_all():
                        g.collective_compute(
                            "AllToAll", OP.bypass,
                            replica_groups=[list(range(NCORE))],
                            ins=[yi_q[0].opt()], outs=[yo_q[0].opt()],
                        ).then_inc(sems["cc"], 1)
                emit("gpsimd", f_a2a_yq0)
                a2ayq0_done = inc("cc", 1)
            te_now = snap("te")
            if n + 2 < NE:
                bv_d[n + 2] = dma("gpsimd", gpp[n % 2],
                                  lambda n=n: bexpr[0:1, (n % 2) * D:(n % 2 + 1) * D],
                                  lambda n=n: bexp_d[n + 2:n + 3, :],
                                  waits=[("te", te_now)])
        te_after_ym = snap("te")

        ysb0_ev = None
        for k in range(0, MCH, 2):
            w = [("cc", a2aya_done), ("te", ym_te_end[5])] if k == 0 else []
            ysb0_ev = dma("gpsimd", "gwc",
                          lambda k=k: ysb0[:, k, :],
                          lambda k=k: yo_h[0][k // 2, :, 0, :, :],
                          waits=w)
        for k in range(1, MCH, 2):
            w = [("cc", a2ayq0_done)] if k == 1 else []
            ysb0_ev = dma("gpsimd", "gwc",
                          lambda k=k: ysb0[0:64, k, :],
                          lambda k=k: yo_q[0][k // 2, :, 0, :, :],
                          waits=w)

        yq2_sy4 = snap("sy4")
        yq2_sy3 = snap("sy3")

        def f_a2a_yq1(g, a=yq2_sy4, c=yq2_sy3):
            g.wait_ge(sems["sy4"], a)
            g.wait_ge(sems["sy3"], c)
            with allow_ici_all_to_all():
                g.collective_compute(
                    "AllToAll", OP.bypass, replica_groups=[list(range(NCORE))],
                    ins=[yi_q[1].opt()], outs=[yo_q[1].opt()],
                ).then_inc(sems["cc"], 1)
        emit("gpsimd", f_a2a_yq1)
        a2ay_done = inc("cc", 1)

        shtp = ["shta", "shtb"]
        sop = ["soa", "sob"]
        ysb_b = [ysb0, ysb1]
        out_wr, a1_of, a2_of = {}, {}, {}
        yb0_last = None
        for k in range(1, MCH, 2):
            w = [("cc", a2ay_done)] if k == 1 else []
            yb0_last = dma("sync", "spk",
                           lambda k=k: ysb0[64:128, k, :],
                           lambda k=k: yo_q[1][k // 2, :, 0, :, :],
                           waits=w)
        yb1_last = None
        for k in range(MCH):
            if k % 2 == 0:
                w = [("cc", a2aya_done), ("te", te_after_ym)] if k == 0 else []
                yb1_last = dma("sync", "s6",
                               lambda k=k: ysb1[:, k, :],
                               lambda k=k: yo_h[0][k // 2, :, 1, :, :],
                               waits=w)
            else:
                w = [("cc", a2ayq0_done)] if k == 1 else []
                yb1_last = dma("sync", "s6",
                               lambda k=k: ysb1[0:64, k, :],
                               lambda k=k: yo_q[0][k // 2, :, 1, :, :],
                               waits=w)
        for k in range(1, MCH, 2):
            w = [("cc", a2ay_done)] if k == 1 else []
            yb1_last = dma("sync", "s6",
                           lambda k=k: ysb1[64:128, k, :],
                           lambda k=k: yo_q[1][k // 2, :, 1, :, :],
                           waits=w)

        eth_sem = {(0, 0): "str", (0, 1): "sp", (1, 0): "s5b", (1, 1): "str"}

        def emit_eth(b, hf, first_waits):
            last = None
            for k in range(MCH):
                last = dma("scalar", eth_sem[(b, hf)],
                           lambda b=b, hf=hf, k=k: eth[:, (2 * b + hf) % 2, k, :],
                           lambda b=b, hf=hf, k=k:
                           e_d[b, hf * 1024:(hf + 1) * 1024,
                               k * 128:(k + 1) * 128],
                           waits=(first_waits if k == 0 else []), transpose=True)
            return last

        eth_of = {(0, 0): emit_eth(0, 0, [("te", te_after_ym)]),
                  (0, 1): emit_eth(0, 1, [])}
        ent_w = {(0, 0): [(eth_sem[(0, 0)], eth_of[(0, 0)]), ("gwc", ysb0_ev),
                          ("spk", yb0_last)],
                 (0, 1): [(eth_sem[(0, 1)], eth_of[(0, 1)])]}
        for b in range(B):
            gi4 = 0
            for mc in range(MCH):
                i8 = b * MCH + mc
                pb = mc % 2
                xv = dma("sync", sxp[pb], lambda pb=pb: xt[:, pb, :],
                         lambda b=b, mc=mc: x_d[b, mc * 128:(mc + 1) * 128, :],
                         waits=[("ve", a1_of[i8 - 2]) if i8 >= 2 else
                                ("ve", snap("ve"))])
                hv = dma("scalar", shtp[pb], lambda pb=pb: htt[:, pb, :],
                         lambda b=b, mc=mc: h_d[b, :, mc * 128:(mc + 1) * 128],
                         waits=[("sha", p3_sha), ("shb", p3_shb),
                                ("ve", a2_of[i8 - 2]) if i8 >= 2 else None],
                         transpose=True)
                ob = pb * 2
                prev_out = out_wr.get(i8 - 2)
                for et in range(2):
                    bank = 4 + (gi4 % 4)
                    gi4 += 1
                    hb = (2 * b + mc // 8) % 2
                    mms = [(lambda kc=kc, mc=mc, hb=hb:
                            eth[:, hb, kc, (mc % 8) * 128:(mc % 8 + 1) * 128],
                            lambda kc=kc, et=et, b=b:
                            ysb_b[b][:, kc, et * 512:(et + 1) * 512])
                           for kc in range(MCH)]
                    if mc % 8 == 0 and et == 0:
                        barrier("tensor", ent_w[(b, mc // 8)])
                    mm_group(bank, mms)
                    consume(bank, "vector",
                            lambda e, bank=bank, b=b, mc=mc, et=et, ob=ob:
                            e.tensor_scalar(obuf[:, ob, et * 512:(et + 1) * 512],
                                            psa[bank][:, :],
                                            dcr[:, b * 16 + mc:b * 16 + mc + 1],
                                            None, OP.mult),
                            "ve",
                            waits=([(sop[pb], prev_out)] if (et == 0 and prev_out)
                                   else []))
                a1_of[i8] = ew("vector", "ve",
                               lambda e, ob=ob, pb=pb: e.tensor_add(
                                   obuf[:, ob + 1, :], obuf[:, ob, :], xt[:, pb, :]),
                               waits=[(sxp[pb], xv)])
                a2_of[i8] = ew("vector", "ve",
                               lambda e, ob=ob, pb=pb: e.tensor_add(
                                   obuf[:, ob, :], obuf[:, ob + 1, :], htt[:, pb, :]),
                               waits=[(shtp[pb], hv)])
                out_wr[i8] = dma("sync", sop[pb],
                                 lambda b=b, mc=mc:
                                 out_d[b, mc * 128:(mc + 1) * 128, :],
                                 lambda ob=ob: obuf[:, ob, :],
                                 waits=[("ve", a2_of[i8])])
                if b == 0 and mc == 9:
                    eth_of[(1, 0)] = emit_eth(1, 0, [("te", snap("te"))])
                    ent_w[(1, 0)] = [(eth_sem[(1, 0)], eth_of[(1, 0)]),
                                     ("s6", yb1_last)]
                if b == 0 and mc == 15:
                    eth_of[(1, 1)] = emit_eth(1, 1, [("te", snap("te"))])
                    ent_w[(1, 1)] = [(eth_sem[(1, 1)], eth_of[(1, 1)])]

        barrier("sync", [("soa", cnt["soa"]), ("sob", cnt["sob"])])

        @block.sync
        def _(sync):
            for f in plan["sync"]:
                f(sync)

        @block.scalar
        def _(scalar):
            for f in plan["scalar"]:
                f(scalar)

        @block.vector
        def _(vector):
            for f in plan["vector"]:
                f(vector)

        @block.tensor
        def _(tensor):
            for f in plan["tensor"]:
                f(tensor)

        @block.gpsimd
        def _(gpsimd):
            for f in plan["gpsimd"]:
                f(gpsimd)

    return nc


_NC_CACHE = []


def kernel(L0_attn_x, L0_attn_W_in, L0_attn_b_in, L0_attn_W_gate, L0_attn_b_gate,
           L0_attn_forget_lambda, L0_ffn_phi, L0_ffn_W, L0_ffn_b):
    x = np.ascontiguousarray(np.asarray(L0_attn_x, dtype=np.float32))
    W_in = np.asarray(L0_attn_W_in, dtype=np.float32)
    W_gate = np.asarray(L0_attn_W_gate, dtype=np.float32)
    b_in = np.asarray(L0_attn_b_in, dtype=np.float32)
    b_gate = np.asarray(L0_attn_b_gate, dtype=np.float32)
    lam = np.asarray(L0_attn_forget_lambda, dtype=np.float32)
    phi = np.ascontiguousarray(np.asarray(L0_ffn_phi, dtype=np.float32).reshape(D, NP))
    W = np.ascontiguousarray(np.asarray(L0_ffn_W, dtype=np.float32))
    bb = np.ascontiguousarray(np.asarray(L0_ffn_b, dtype=np.float32))

    wint = np.ascontiguousarray(W_in.T)
    wgt = np.ascontiguousarray(W_gate.T)
    binh = np.ascontiguousarray(b_in.reshape(DCH, 128).T)
    bgth = np.ascontiguousarray(b_gate.reshape(DCH, 128).T)
    lamh = np.ascontiguousarray(lam.reshape(DCH, 128).T)

    if not _NC_CACHE:
        _NC_CACHE.append(build())
    nc = _NC_CACHE[0]

    in_maps = []
    for c in range(NCORE):
        in_maps.append({
            "x": np.ascontiguousarray(x[2 * c:2 * c + 2]),
            "wint": wint,
            "wgt": wgt,
            "phi": phi,
            "wexp": np.ascontiguousarray(W[NE * c:NE * (c + 1)]),
            "binh": binh,
            "bgth": bgth,
            "lamh": lamh,
            "bexp": np.ascontiguousarray(bb[NE * c:NE * (c + 1)]),
        })
    res = run_bass_kernel_spmd(nc, in_maps, core_ids=list(range(NCORE)))
    out = np.concatenate([res.results[c]["out"] for c in range(NCORE)], axis=0)
    return out
